# version 01
# baseline (speedup 1.0000x reference)
from contextlib import ExitStack

import numpy as np
import ml_dtypes
import concourse.bass as bass
import concourse.mybir as mybir
from concourse.bass_utils import run_bass_kernel_spmd

F32 = mybir.dt.float32
BF16 = mybir.dt.bfloat16
AF = mybir.ActivationFunctionType
ALU = mybir.AluOpType
AX = mybir.AxisListType
BF = ml_dtypes.bfloat16

NCORES = 8
B, T, D = 2, 8192, 1024
TC = T // 4
MAGIC = 12582912.0

ENGS = ("sync", "gpsimd", "vector", "scalar", "tensor")


class Res:
    __slots__ = ("name", "writer", "readers")

    def __init__(self, name=""):
        self.name = name
        self.writer = None
        self.readers = []


class Prog:

    def __init__(self, nc, blk):
        self.nc, self.blk = nc, blk
        self.sem = {}
        self.cnt = {e: 0 for e in ENGS}
        self.clock = {e: {f: 0 for f in ENGS} for e in ENGS}
        self._cms = []
        for e in ENGS:
            cm = nc.semaphore("sem_" + e)
            self.sem[e] = cm.__enter__()
            self._cms.append(cm)
        cm = nc.semaphore("sem_aux")
        self.aux = cm.__enter__()
        self._cms.append(cm)
        self.auxc = [0]

    def chain(self, e):
        aux, auxc = self.aux, self.auxc
        def ch(instr):
            auxc[0] += 1
            instr.then_inc(aux, 1)
            e.wait_ge(aux, auxc[0])
            return instr
        return ch

    def step(self, eng, reads=(), writes=(), dma=False):
        def deco(fn):
            needs = {}

            def need(tok, raw=False):
                if tok is None:
                    return
                e2, c2, _clk = tok
                if e2 == eng and not (raw or eng in ("sync", "gpsimd")):
                    return
                if self.clock[eng][e2] >= c2:
                    return
                needs[e2] = max(needs.get(e2, 0), c2)

            for r in reads:
                need(r.writer, raw=True)
            for w in writes:
                need(w.writer)
                for t in w.readers:
                    need(t)

            emitted = {"n": 0}
            prog = self

            section = getattr(self.blk, eng)

            def body(e):
                for e2, c2 in sorted(needs.items()):
                    e.wait_ge(prog.sem[e2], c2)
                if dma:
                    def dd(out, in_, **kw):
                        ins = e.dma_start(out=out, in_=in_, **kw)
                        ins.then_inc(prog.sem[eng], 16)
                        emitted["n"] += 16
                        return ins
                    fn(e, dd)
                else:
                    last = fn(e)
                    assert last is not None, "compute step must return last instr"
                    last.then_inc(prog.sem[eng], 1)
                    emitted["n"] += 1
            section(body)

            for e2, c2 in needs.items():
                if c2 > self.clock[eng][e2]:
                    self.clock[eng][e2] = c2
            self.cnt[eng] += emitted["n"]
            tok = (eng, self.cnt[eng], dict(self.clock[eng]))
            for w in writes:
                w.writer = tok
                w.readers = []
            for r in reads:
                r.readers.append(tok)
            return tok
        return deco

    def drain(self):
        cnts = dict(self.cnt)
        @self.blk.gpsimd
        def _(e):
            for e2 in ENGS:
                if cnts[e2] > 0:
                    e.wait_ge(self.sem[e2], cnts[e2])



def emit_quant(v, x_ap, out_ap, scr, nb, ch):
    P = 128
    F = nb * 16
    x3 = x_ap.rearrange("p (n b) -> p n b", b=16)
    z3 = scr['z'][:, :F].rearrange("p (n b) -> p n b", b=16)
    v.tensor_reduce(scr['amax'][:, :nb], x3, AX.X, ALU.max, apply_absolute_value=True)
    ch(v.tensor_scalar(scr['s6'][:, :nb], scr['amax'][:, :nb], 1.0 / 6.0, 1e-12, ALU.mult, ALU.max))
    ch(v.reciprocal(scr['r2'][:, :nb], scr['s6'][:, :nb]))
    ch(v.tensor_scalar_mul(scr['r2'][:, :nb], scr['r2'][:, :nb], 2.0))
    r2b = scr['r2'][:, :nb].unsqueeze(2).broadcast_to([P, nb, 16])
    s6b = scr['s6'][:, :nb].unsqueeze(2).broadcast_to([P, nb, 16])
    z = scr['z'][:, :F]
    a = scr['a'][:, :F]
    b = scr['b'][:, :F]
    c = scr['c'][:, :F]
    a3v = a.rearrange("p (n b) -> p n b", b=16)
    v.tensor_tensor(z3, a3v, r2b, ALU.mult)
    v.tensor_scalar(a, z, MAGIC, -MAGIC, ALU.add, ALU.add)
    v.tensor_scalar(b, z, 0.5, MAGIC, ALU.mult, ALU.add)
    v.tensor_scalar(c, b, -(MAGIC + 2.0), 0.0, ALU.add, ALU.max)
    v.tensor_scalar(b, a, 4.0, 0.5, ALU.min, ALU.mult)
    v.scalar_tensor_tensor(a, c, 2.0, b, ALU.min, ALU.add)
    v.tensor_single_scalar(c, z, 10.0, ALU.is_gt)
    v.scalar_tensor_tensor(b, c, 2.0, a, ALU.mult, ALU.add)
    v.tensor_tensor(a, b, scr['sgn'][:, :F], ALU.mult)
    a3 = a.rearrange("p (n b) -> p n b", b=16)
    o3 = out_ap.rearrange("p (n b) -> p n b", b=16)
    return v.scalar_tensor_tensor(o3, a3, 0.0, s6b, ALU.add, ALU.mult)


def emit_ln_stats(v, x_ap, st, F, ch):
    v.reduce_sum(st['s1'], x_ap, AX.X)
    v.tensor_tensor(st['sq'][:, :F], x_ap, x_ap, ALU.mult)
    ch(v.reduce_sum(st['s2'], st['sq'][:, :F], AX.X))
    v.tensor_scalar_mul(st['mu'], st['s1'], 1.0 / F)
    ch(v.tensor_scalar_mul(st['m2'], st['s2'], 1.0 / F))
    ch(v.tensor_tensor(st['t'], st['mu'], st['mu'], ALU.mult))
    ch(v.tensor_tensor(st['var'], st['m2'], st['t'], ALU.subtract))
    return v.tensor_scalar_add(st['vpe'], st['var'], 1e-5)



def build_l0ffn(debug=False):
    nc = bass.Bass("TRN2", target_bir_lowering=False)
    xin = nc.dram_tensor("xin", [TC, D], F32, kind="ExternalInput").ap()
    wg_d = nc.dram_tensor("wg", [2816, D], F32, kind="ExternalInput").ap()
    wu_d = nc.dram_tensor("wu", [2816, D], F32, kind="ExternalInput").ap()
    wd_d = nc.dram_tensor("wd", [D, 2816], F32, kind="ExternalInput").ap()
    idb_d = nc.dram_tensor("idb", [128, 128], BF16, kind="ExternalInput").ap()
    xout = nc.dram_tensor("xout", [TC, D], F32, kind="ExternalOutput").ap()
    hq_hbm = nc.dram_tensor("hqscratch", [TC // 128, 128, 2816], BF16, kind="Internal").ap()
    if debug:
        dbg_n = nc.dram_tensor("dbg_n", [128, 1024], F32, kind="ExternalOutput").ap()
        dbg_xq = nc.dram_tensor("dbg_xq", [128, 1024], F32, kind="ExternalOutput").ap()
        dbg_wgq = nc.dram_tensor("dbg_wgq", [128, 2816], F32, kind="ExternalOutput").ap()
        dbg_h = nc.dram_tensor("dbg_h", [128, 2816], F32, kind="ExternalOutput").ap()
        dbg_hq = nc.dram_tensor("dbg_hq", [128, 2816], F32, kind="ExternalOutput").ap()

    NT = TC // 128
    NI = 2816 // 128

    es = ExitStack()
    sb = lambda *a: es.enter_context(nc.sbuf_tensor(*a))
    ps = lambda *a: es.enter_context(nc.psum_tensor(*a))
    idb = sb("idb_sb", [128, 128], BF16)
    xt = sb("xt", [128, 2, 1408], F32)
    z_ = sb("z", [128, 1408], F32)
    a_ = sb("a", [128, 1408], F32)
    b_ = sb("bq", [128, 1408], F32)
    c_ = sb("cq", [128, 1408], F32)
    sgn_ = sb("sgn", [128, 1408], BF16)
    qout = sb("qout", [128, 2, 1408], BF16)
    nsm = sb("nsm", [128, 16], F32)
    amax_ = sb("amax", [128, 88], F32)
    s6_ = sb("s6", [128, 88], F32)
    r2_ = sb("r2", [128, 88], F32)
    nbuf = sb("nbuf", [128, 2, 1024], F32)
    xo = sb("xo", [128, 2, 1024], F32)
    tp0 = ps("tp0", [128, 128], BF16)
    tp1 = ps("tp1", [128, 128], BF16)
    gps = ps("gps", [128, 512], F32)
    ups = ps("ups", [128, 512], F32)
    dps = ps("dps", [128, 2, 512], F32)
    with nc.Block() as blk:
        p = Prog(nc, blk)
        scr = dict(z=z_, a=a_, b=b_, c=c_, sgn=sgn_, amax=amax_, s6=s6_, r2=r2_)
        st = dict(s1=nsm[:, 0:1], s2=nsm[:, 1:2], mu=nsm[:, 2:3], m2=nsm[:, 3:4],
                  t=nsm[:, 4:5], var=nsm[:, 5:6], vpe=nsm[:, 6:7], std=nsm[:, 7:8],
                  rstd=nsm[:, 8:9], sq=z_)
        r_idb = Res("idb")
        r_xt = [Res("xt0"), Res("xt1")]
        r_scr = Res("scr")
        r_sgn = Res("sgn")
        r_qout = [Res("q0"), Res("q1")]
        r_nbuf = [Res("n0"), Res("n1")]
        r_tp = [Res("tp0"), Res("tp1")]
        r_gps, r_ups = Res("gps"), Res("ups")
        r_dps = [Res("dps0"), Res("dps1")]
        r_xo = [Res("xo0"), Res("xo1")]
        r_hqt = Res("hqTt")
        r_hqhbm = [Res() for _ in range(16)]
        r_hbuf = Res("hbuf")
        r_gs = Res("gs")

        @p.step("sync", writes=[r_idb], dma=True)
        def _(e, dd):
            dd(out=idb[:], in_=idb_d[:, :])

        tp = [tp0, tp1]
        tpi = [0]

        def quant_chain(x_res, x_ap, q_res, q_ap, nb):
            F = nb * 16
            @p.step("scalar", reads=[x_res], writes=[r_sgn, r_scr])
            def _(e):
                e.activation(out=sgn_[:, :F], in_=x_ap, func=AF.Sign)
                return e.activation(out=a_[:, :F], in_=x_ap, func=AF.Abs)
            @p.step("vector", reads=[x_res, r_sgn], writes=[q_res, r_scr])
            def _(e):
                return emit_quant(e, x_ap, q_ap, scr, nb, p.chain(e))

        def transpose_to(q_res, q_ap_fn, nblk, dst_res, dst_ap_fn):
            for d in range(nblk):
                i = tpi[0] % 2
                tpi[0] += 1
                @p.step("tensor", reads=[q_res, r_idb], writes=[r_tp[i]])
                def _(e, d=d, i=i):
                    return e.transpose(tp[i][:], q_ap_fn(d), idb[:])
                @p.step("vector", reads=[r_tp[i]], writes=[dst_res])
                def _(e, d=d, i=i):
                    return e.tensor_copy(dst_ap_fn(d), tp[i][:])

        es1 = ExitStack()
        sb1 = lambda *a: es1.enter_context(nc.sbuf_tensor(*a))
        wgT = sb1("wgT", [128, 8, 2816], BF16)
        wuT = sb1("wuT", [128, 8, 2816], BF16)
        xqTt = sb1("xqTt", [128, 2, 1024], BF16)
        hqTt = sb1("hqTt", [128, 2816], BF16)
        hbuf = sb1("hbuf", [128, 2816], F32)
        gs = sb1("gs", [128, 512], BF16)
        if True:
            r_wgT, r_wuT = Res("wgT"), Res("wuT")
            r_xqTt = [Res("xqTt0"), Res("xqTt1")]

            for wdram, wT, wres in ((wg_d, wgT, r_wgT), (wu_d, wuT, r_wuT)):
                for i in range(NI):
                    bi = i % 2
                    @p.step("sync", writes=[r_xt[bi]], dma=True)
                    def _(e, dd, i=i, bi=bi, wdram=wdram):
                        dd(out=xt[:, bi, :1024], in_=wdram[i * 128:(i + 1) * 128, :])
                    quant_chain(r_xt[bi], xt[:, bi, :1024], r_qout[bi],
                                qout[:, bi, :1024], 64)
                    transpose_to(r_qout[bi], (lambda d, bi=bi: qout[:, bi, d * 128:(d + 1) * 128]),
                                 8, wres,
                                 (lambda d, i=i, wT=wT: wT[:, d, i * 128:(i + 1) * 128]))

            if debug:
                @p.step("gpsimd", reads=[r_wgT], dma=True)
                def _(e, dd):
                    dd(out=dbg_wgq[:, :], in_=wgT[:, 0, :])
            otiles = [(0, 512), (512, 512), (1024, 512), (1536, 512), (2048, 512), (2560, 256)]
            for t in range(NT):
                bi = t % 2
                @p.step("sync", writes=[r_xt[bi]], dma=True)
                def _(e, dd, t=t, bi=bi):
                    dd(out=xt[:, bi, :1024], in_=xin[t * 128:(t + 1) * 128, :])
                @p.step("vector", reads=[r_xt[bi]], writes=[r_scr])
                def _(e, bi=bi):
                    return emit_ln_stats(e, xt[:, bi, :1024], st, 1024, p.chain(e))
                @p.step("scalar", reads=[r_scr], writes=[r_scr])
                def _(e):
                    return e.activation(out=st['std'], in_=st['vpe'], func=AF.Sqrt)
                @p.step("vector", reads=[r_xt[bi], r_scr], writes=[r_nbuf[bi], r_scr])
                def _(e, bi=bi):
                    p.chain(e)(e.reciprocal(st['rstd'], st['std']))
                    return e.tensor_scalar(nbuf[:, bi, :], xt[:, bi, :1024],
                                           st['mu'], st['rstd'], ALU.subtract, ALU.mult)
                if debug and t == 0:
                    @p.step("gpsimd", reads=[r_nbuf[bi]], dma=True)
                    def _(e, dd, bi=bi):
                        dd(out=dbg_n[:, :], in_=nbuf[:, bi, :])
                quant_chain(r_nbuf[bi], nbuf[:, bi, :], r_qout[bi], qout[:, bi, :1024], 64)
                if debug and t == 0:
                    @p.step("gpsimd", reads=[r_qout[bi]], dma=True)
                    def _(e, dd, bi=bi):
                        dd(out=dbg_xq[:, :], in_=qout[:, bi, :1024])
                transpose_to(r_qout[bi], (lambda d, bi=bi: qout[:, bi, d * 128:(d + 1) * 128]),
                             8, r_xqTt[bi],
                             (lambda d, bi=bi: xqTt[:, bi, d * 128:(d + 1) * 128]))
                for (o0, ow) in otiles:
                    @p.step("tensor", reads=[r_xqTt[bi], r_wgT, r_wuT], writes=[r_gps, r_ups])
                    def _(e, bi=bi, o0=o0, ow=ow):
                        for d in range(8):
                            e.matmul(gps[:, :ow], xqTt[:, bi, d * 128:(d + 1) * 128],
                                     wgT[:, d, o0:o0 + ow], start=(d == 0), stop=(d == 7))
                        for d in range(8):
                            last = e.matmul(ups[:, :ow], xqTt[:, bi, d * 128:(d + 1) * 128],
                                            wuT[:, d, o0:o0 + ow], start=(d == 0), stop=(d == 7))
                        return last
                    @p.step("scalar", reads=[r_gps], writes=[r_gs])
                    def _(e, ow=ow):
                        return e.activation(out=gs[:, :ow], in_=gps[:, :ow], func=AF.Silu)
                    @p.step("vector", reads=[r_gs, r_ups], writes=[r_hbuf])
                    def _(e, o0=o0, ow=ow):
                        return e.tensor_tensor(hbuf[:, o0:o0 + ow], ups[:, :ow],
                                               gs[:, :ow], ALU.mult)
                for hf in range(2):
                    h0 = hf * 1408
                    quant_chain(r_hbuf, hbuf[:, h0:h0 + 1408], r_qout[hf],
                                qout[:, hf, :1408], 88)
                    transpose_to(r_qout[hf], (lambda d, hf=hf: qout[:, hf, d * 128:(d + 1) * 128]),
                                 11, r_hqt,
                                 (lambda d, h0=h0: hqTt[:, h0 + d * 128: h0 + (d + 1) * 128]))
                if debug and t == 0:
                    @p.step("gpsimd", reads=[r_hbuf], dma=True)
                    def _(e, dd):
                        dd(out=dbg_h[:, :], in_=hbuf[:, :])
                    @p.step("gpsimd", reads=[r_hqt], dma=True)
                    def _(e, dd):
                        dd(out=dbg_hq[:, :], in_=hqTt[:, :])
                @p.step("gpsimd", reads=[r_hqt], writes=[r_hqhbm[t]], dma=True)
                def _(e, dd, t=t):
                    dd(out=hq_hbm[t], in_=hqTt[:, :])

        es1.close()
        es2 = ExitStack()
        wdT = es2.enter_context(nc.sbuf_tensor("wdT", [128, NI, 1024], BF16))
        hqin = es2.enter_context(nc.sbuf_tensor("hqin", [128, 2, 2816], BF16))
        if True:
            r_wdT = Res("wdT")
            r_hqin = [Res("hq0"), Res("hq1")]
            for i in range(8):
                for hf in range(2):
                    h0 = hf * 1408
                    bi = (2 * i + hf) % 2
                    @p.step("sync", writes=[r_xt[bi]], dma=True)
                    def _(e, dd, i=i, h0=h0, bi=bi):
                        dd(out=xt[:, bi, :1408], in_=wd_d[i * 128:(i + 1) * 128, h0:h0 + 1408])
                    quant_chain(r_xt[bi], xt[:, bi, :1408], r_qout[bi],
                                qout[:, bi, :1408], 88)
                    transpose_to(r_qout[bi], (lambda d, bi=bi: qout[:, bi, d * 128:(d + 1) * 128]),
                                 11, r_wdT,
                                 (lambda d, i=i, h0=h0: wdT[:, h0 // 128 + d, i * 128:(i + 1) * 128]))
            for t in range(NT):
                bi = t % 2
                @p.step("sync", reads=[r_hqhbm[t]], writes=[r_hqin[bi]], dma=True)
                def _(e, dd, t=t, bi=bi):
                    dd(out=hqin[:, bi, :], in_=hq_hbm[t])
                @p.step("sync", writes=[r_xt[bi]], dma=True)
                def _(e, dd, t=t, bi=bi):
                    dd(out=xt[:, bi, :1024], in_=xin[t * 128:(t + 1) * 128, :])
                for oi, (o0, ow) in enumerate([(0, 512), (512, 512)]):
                    @p.step("tensor", reads=[r_hqin[bi], r_wdT], writes=[r_dps[oi]])
                    def _(e, bi=bi, o0=o0, ow=ow, oi=oi):
                        for i in range(NI):
                            last = e.matmul(dps[:, oi, :ow], hqin[:, bi, i * 128:(i + 1) * 128],
                                            wdT[:, i, o0:o0 + ow], start=(i == 0), stop=(i == NI - 1))
                        return last
                    @p.step("vector", reads=[r_dps[oi], r_xt[bi]], writes=[r_xo[bi]])
                    def _(e, bi=bi, o0=o0, ow=ow, oi=oi):
                        return e.tensor_tensor(xo[:, bi, o0:o0 + ow], dps[:, oi, :ow],
                                               xt[:, bi, o0:o0 + ow], ALU.add)
                @p.step("gpsimd", reads=[r_xo[bi]], writes=[], dma=True)
                def _(e, dd, t=t, bi=bi):
                    dd(out=xout[t * 128:(t + 1) * 128, :], in_=xo[:, bi, :])
        es2.close()
        p.drain()
    es.close()
    return nc





def build_l0attn(debug=False):
    nc = bass.Bass("TRN2", target_bir_lowering=False)
    NTE = 18
    xin = nc.dram_tensor("xin", [NTE * 128, D], F32, kind="ExternalInput").ap()
    wqkv_d = nc.dram_tensor("wqkv", [D, 3 * D], F32, kind="ExternalInput").ap()
    wout_d = nc.dram_tensor("wout", [D, D], F32, kind="ExternalInput").ap()
    bout_d = nc.dram_tensor("bout", [1, D], F32, kind="ExternalInput").ap()
    cos_d = nc.dram_tensor("cosd", [NTE * 128, 32], F32, kind="ExternalInput").ap()
    sin_d = nc.dram_tensor("sind", [NTE * 128, 32], F32, kind="ExternalInput").ap()
    msk_d = nc.dram_tensor("mskd", [16 * 128, 384], BF16, kind="ExternalInput").ap()
    idb_d = nc.dram_tensor("idb", [128, 128], BF16, kind="ExternalInput").ap()
    xout = nc.dram_tensor("xout", [TC, D], F32, kind="ExternalOutput").ap()
    if debug:
        dbg_q = nc.dram_tensor("dbg_q", [128, D], F32, kind="ExternalOutput").ap()
        dbg_k = nc.dram_tensor("dbg_k", [128, D], F32, kind="ExternalOutput").ap()
        dbg_v = nc.dram_tensor("dbg_v", [128, D], F32, kind="ExternalOutput").ap()
        dbg_p = nc.dram_tensor("dbg_p", [128, 384], F32, kind="ExternalOutput").ap()
        dbg_at = nc.dram_tensor("dbg_at", [128, 8 * 128], F32, kind="ExternalOutput").ap()

    aT_hbm = nc.dram_tensor("aTscratch", [8, 128, TC], BF16, kind="Internal").ap()
    es = ExitStack()
    sb = lambda *a: es.enter_context(nc.sbuf_tensor(*a))
    ps = lambda *a: es.enter_context(nc.psum_tensor(*a))
    idb = sb("idb_sb", [128, 128], BF16)
    bout_sb = sb("bout_sb", [1, D], BF16)
    ones_sb = sb("ones_sb", [1, 128], BF16)
    cos_sb = sb("cos_sb", [128, NTE, 32], BF16)
    sin_sb = sb("sin_sb", [128, NTE, 32], BF16)
    xt = sb("xt", [128, 2, D], F32)
    nbf = sb("nbf", [128, 2, D], BF16)
    ntS = sb("ntS", [128, 2, D], BF16)
    qt = sb("qt", [128, 2, D], BF16)
    kt = sb("kt", [128, 2, D], BF16)
    rt = sb("rt", [128, 2048], F32)
    vbuf = sb("vbuf", [128, NTE, D], BF16)
    qT = sb("qT", [128, 8, NTE * 128], BF16)
    kT = sb("kT", [128, 8, NTE * 128], BF16)
    pb = sb("pb", [128, 384], BF16)
    pT = sb("pT", [128, 384], BF16)
    nsm = sb("nsm2", [128, 16], F32)
    xo = sb("xo", [128, 2, D], F32)
    es1 = ExitStack()
    wq_sb = es1.enter_context(nc.sbuf_tensor("wq_sb", [128, 8, 3 * D], BF16))
    tpA = ps("tpA", [128, 128], BF16)
    tpB = ps("tpB", [128, 128], BF16)
    mmA = ps("mmA", [128, 512], F32)
    mmB = ps("mmB", [128, 512], F32)
    scp = ps("scp", [128, 384], F32)
    ops_ = ps("ops", [64, 128], F32)
    with nc.Block() as blk:
        p = Prog(nc, blk)
        st = dict(s1=nsm[:, 0:1], s2=nsm[:, 1:2], mu=nsm[:, 2:3], m2=nsm[:, 3:4],
                  t=nsm[:, 4:5], var=nsm[:, 5:6], vpe=nsm[:, 6:7], std=nsm[:, 7:8],
                  rstd=nsm[:, 8:9], mx=nsm[:, 9:10], bias=nsm[:, 10:11],
                  rs=nsm[:, 11:12], rr=nsm[:, 12:13], sq=rt)
        r_const = Res("const")
        r_xt = [Res(), Res()]
        r_nbf = [Res(), Res()]
        r_ntS = [Res(), Res()]
        r_qt = [Res(), Res()]
        r_kt = [Res(), Res()]
        r_rt = Res("rt")
        r_vbuf = Res("vbuf")
        r_qT, r_kT = Res("qT"), Res("kT")
        r_pb, r_pT = Res("pb"), Res("pT")
        r_nsm = Res("nsm")
        r_tp = [Res(), Res()]
        r_mm = [Res(), Res()]
        r_scp, r_ops = Res("scp"), Res("ops")
        r_xo = [Res(), Res()]
        tp = [tpA, tpB]
        mm = [mmA, mmB]

        @p.step("sync", writes=[r_const], dma=True)
        def _(e, dd):
            dd(out=idb[:], in_=idb_d[:, :])

        @p.step("gpsimd", writes=[r_const], dma=True)
        def _(e, dd):
            dd(out=cos_sb[:], in_=cos_d.rearrange("(t p) c -> p t c", p=128))
            dd(out=sin_sb[:], in_=sin_d.rearrange("(t p) c -> p t c", p=128))
            dd(out=wq_sb[:], in_=wqkv_d.rearrange("(c p) o -> p c o", p=128))
            dd(out=bout_sb[:], in_=bout_d[:, :])

        @p.step("vector", writes=[r_const])
        def _(e):
            return e.memset(ones_sb[:], 1.0)

        for t in range(NTE):
            bi = t % 2
            @p.step("sync", writes=[r_xt[bi]], dma=True)
            def _(e, dd, t=t):
                dd(out=xt[:, t % 2, :], in_=xin[t * 128:(t + 1) * 128, :])
            @p.step("vector", reads=[r_xt[bi]], writes=[r_nsm, r_rt])
            def _(e, bi=bi):
                return emit_ln_stats(e, xt[:, bi, :], st, 1024, p.chain(e))
            @p.step("scalar", reads=[r_nsm], writes=[r_nsm])
            def _(e):
                return e.activation(out=st['std'], in_=st['vpe'], func=AF.Sqrt)
            @p.step("vector", reads=[r_xt[bi], r_nsm], writes=[r_nbf[bi], r_nsm])
            def _(e, bi=bi):
                p.chain(e)(e.reciprocal(st['rstd'], st['std']))
                return e.tensor_scalar(nbf[:, bi, :], xt[:, bi, :],
                                       st['mu'], st['rstd'], ALU.subtract, ALU.mult)
            for d in range(8):
                ti = d % 2
                @p.step("tensor", reads=[r_nbf[bi], r_const], writes=[r_tp[ti]])
                def _(e, bi=bi, d=d, ti=ti):
                    return e.transpose(tp[ti][:], nbf[:, bi, d * 128:(d + 1) * 128], idb[:])
                @p.step("vector", reads=[r_tp[ti]], writes=[r_ntS[bi]])
                def _(e, bi=bi, d=d, ti=ti):
                    return e.tensor_copy(ntS[:, bi, d * 128:(d + 1) * 128], tp[ti][:])
            for s6 in range(6):
                mi = s6 % 2
                @p.step("tensor", reads=[r_ntS[bi], r_const], writes=[r_mm[mi]])
                def _(e, bi=bi, s6=s6, mi=mi):
                    for d in range(8):
                        last = e.matmul(mm[mi][:], ntS[:, bi, d * 128:(d + 1) * 128],
                                        wq_sb[:, d, s6 * 512:(s6 + 1) * 512],
                                        start=(d == 0), stop=(d == 7))
                    return last
                dst = [(qt, r_qt[bi], 0), (qt, r_qt[bi], 512), (kt, r_kt[bi], 0),
                       (kt, r_kt[bi], 512), (None, r_vbuf, 0), (None, r_vbuf, 512)][s6]
                @p.step("vector", reads=[r_mm[mi]], writes=[dst[1]])
                def _(e, bi=bi, s6=s6, mi=mi, dst=dst, t=t):
                    if dst[0] is None:
                        return e.tensor_copy(vbuf[:, t, dst[2]:dst[2] + 512], mm[mi][:])
                    return e.tensor_copy(dst[0][:, bi, dst[2]:dst[2] + 512], mm[mi][:])
            for (buf, rres) in ((qt, r_qt[bi]), (kt, r_kt[bi])):
                @p.step("vector", reads=[rres, r_const], writes=[rres, r_rt])
                def _(e, bi=bi, buf=buf, t=t):
                    v3 = buf[:, bi, :].rearrange("p (h x) -> p h x", x=64)
                    x1, x2 = v3[:, :, 0:32], v3[:, :, 32:64]
                    cb = cos_sb[:, t, :].unsqueeze(1).broadcast_to([128, 16, 32])
                    sbq = sin_sb[:, t, :].unsqueeze(1).broadcast_to([128, 16, 32])
                    t1 = rt[:, 0:512].rearrange("p (h x) -> p h x", x=32)
                    t2 = rt[:, 512:1024].rearrange("p (h x) -> p h x", x=32)
                    t3 = rt[:, 1024:1536].rearrange("p (h x) -> p h x", x=32)
                    t4 = rt[:, 1536:2048].rearrange("p (h x) -> p h x", x=32)
                    e.tensor_tensor(t1, x1, cb, ALU.mult)
                    e.tensor_tensor(t2, x2, sbq, ALU.mult)
                    e.tensor_tensor(t3, x2, cb, ALU.mult)
                    e.tensor_tensor(t4, x1, sbq, ALU.mult)
                    e.tensor_tensor(x1, t1, t2, ALU.subtract)
                    return e.tensor_tensor(x2, t3, t4, ALU.add)
            if debug and t == 2:
                @p.step("gpsimd", reads=[r_qt[bi], r_kt[bi], r_vbuf], dma=True)
                def _(e, dd, bi=bi, t=t):
                    dd(out=dbg_q[:, :], in_=qt[:, bi, :])
                    dd(out=dbg_k[:, :], in_=kt[:, bi, :])
                    dd(out=dbg_v[:, :], in_=vbuf[:, t, :])
            for (buf, rres, dstT, dres) in ((qt, r_qt[bi], qT, r_qT), (kt, r_kt[bi], kT, r_kT)):
                for d in range(8):
                    ti = d % 2
                    @p.step("tensor", reads=[rres, r_const], writes=[r_tp[ti]])
                    def _(e, bi=bi, d=d, ti=ti, buf=buf):
                        return e.transpose(tp[ti][:], buf[:, bi, d * 128:(d + 1) * 128], idb[:])
                    @p.step("vector", reads=[r_tp[ti]], writes=[dres])
                    def _(e, d=d, ti=ti, t=t, dstT=dstT):
                        return e.tensor_copy(dstT[:, d, t * 128:(t + 1) * 128], tp[ti][:])

        es1.close()
        es2 = ExitStack()
        msk_sb = es2.enter_context(nc.sbuf_tensor("msk_sb", [128, 16, 384], BF16))
        wo_sb = es2.enter_context(nc.sbuf_tensor("wo_sb", [128, 8, D], BF16))
        aTs = es2.enter_context(nc.sbuf_tensor("aTs", [128, 8, 128], BF16))
        aTin = es2.enter_context(nc.sbuf_tensor("aTin", [128, 2, 8, 128], BF16))
        r_aTs = Res("aTs")
        r_aTin = [Res(), Res()]
        r_aThbm = [Res() for _ in range(16)]

        @p.step("sync", writes=[r_const], dma=True)
        def _(e, dd):
            dd(out=msk_sb[:], in_=msk_d.rearrange("(t p) c -> p t c", p=128))

        @p.step("gpsimd", writes=[r_const], dma=True)
        def _(e, dd):
            dd(out=wo_sb[:], in_=wout_d.rearrange("(c p) o -> p c o", p=128))

        for qb in range(16):
            for h in range(16):
                hp0 = (h % 2) * 64
                hc = h // 2
                @p.step("tensor", reads=[r_qT, r_kT], writes=[r_scp])
                def _(e, h=h, qb=qb, hp0=hp0, hc=hc):
                    return e.matmul(scp[:],
                                    qT[hp0:hp0 + 64, hc, (qb + 1) * 128:(qb + 2) * 128],
                                    kT[hp0:hp0 + 64, hc, qb * 128:qb * 128 + 384],
                                    start=True, stop=True)
                @p.step("vector", reads=[r_scp], writes=[r_nsm])
                def _(e):
                    ch = p.chain(e)
                    ch(e.reduce_max(st['mx'], scp[:], AX.X))
                    return e.tensor_scalar_mul(st['bias'], st['mx'], -0.125)
                @p.step("scalar", reads=[r_scp, r_nsm], writes=[r_pb])
                def _(e):
                    return e.activation(out=pb[:], in_=scp[:], func=AF.Exp,
                                        bias=st['bias'], scale=0.125)
                @p.step("vector", reads=[r_pb, r_const], writes=[r_pb, r_nsm])
                def _(e, qb=qb):
                    ch = p.chain(e)
                    e.tensor_tensor(pb[:], pb[:], msk_sb[:, qb, :], ALU.mult)
                    ch(e.reduce_sum(st['rs'], pb[:], AX.X))
                    ch(e.reciprocal(st['rr'], st['rs']))
                    return e.tensor_scalar_mul(pb[:], pb[:], st['rr'])
                if debug and qb == 4 and h == 3:
                    @p.step("gpsimd", reads=[r_pb], dma=True)
                    def _(e, dd):
                        dd(out=dbg_p[:, :], in_=pb[:])
                for j in range(3):
                    ti = j % 2
                    @p.step("tensor", reads=[r_pb, r_const], writes=[r_tp[ti]])
                    def _(e, j=j, ti=ti):
                        return e.transpose(tp[ti][:], pb[:, j * 128:(j + 1) * 128], idb[:])
                    @p.step("vector", reads=[r_tp[ti]], writes=[r_pT])
                    def _(e, j=j, ti=ti):
                        return e.tensor_copy(pT[:, j * 128:(j + 1) * 128], tp[ti][:])
                @p.step("tensor", reads=[r_pT, r_vbuf], writes=[r_ops])
                def _(e, h=h, qb=qb):
                    for j in range(3):
                        last = e.matmul(ops_[:], vbuf[:, qb + j, h * 64:(h + 1) * 64],
                                        pT[:, j * 128:(j + 1) * 128],
                                        start=(j == 0), stop=(j == 2))
                    return last
                @p.step("vector", reads=[r_ops], writes=[r_aTs])
                def _(e, h=h, hp0=hp0, hc=hc):
                    return e.tensor_copy(aTs[hp0:hp0 + 64, hc, :], ops_[:])
            @p.step("gpsimd", reads=[r_aTs], writes=[r_aThbm[qb]], dma=True)
            def _(e, dd, qb=qb):
                dd(out=aT_hbm[:, :, qb * 128:(qb + 1) * 128].rearrange("c p q -> p c q"),
                   in_=aTs[:])
            if debug and qb == 4:
                @p.step("gpsimd", reads=[r_aTs], dma=True)
                def _(e, dd):
                    dd(out=dbg_at.rearrange("p (c q) -> p c q", q=128), in_=aTs[:])

        for t in range(16):
            bi = t % 2
            @p.step("sync", reads=[r_aThbm[t]], writes=[r_xt[bi], r_aTin[bi]], dma=True)
            def _(e, dd, t=t, bi=bi):
                dd(out=xt[:, bi, :], in_=xin[(t + 1) * 128:(t + 2) * 128, :])
                dd(out=aTin[:, bi, :, :],
                   in_=aT_hbm[:, :, t * 128:(t + 1) * 128].rearrange("c p q -> p c q"))
            for oi in range(2):
                mi = oi % 2
                @p.step("tensor", reads=[r_aTin[bi], r_const], writes=[r_mm[mi]])
                def _(e, t=t, oi=oi, mi=mi, bi=bi):
                    for d in range(8):
                        e.matmul(mm[mi][:], aTin[:, bi, d, :],
                                 wo_sb[:, d, oi * 512:(oi + 1) * 512],
                                 start=(d == 0), stop=False)
                    return e.matmul(mm[mi][:], ones_sb[:, 0:128],
                                    bout_sb[:, oi * 512:(oi + 1) * 512],
                                    start=False, stop=True)
                @p.step("vector", reads=[r_mm[mi], r_xt[bi]], writes=[r_xo[bi]])
                def _(e, t=t, oi=oi, mi=mi, bi=bi):
                    return e.tensor_tensor(xo[:, bi, oi * 512:(oi + 1) * 512], mm[mi][:],
                                           xt[:, bi, oi * 512:(oi + 1) * 512], ALU.add)
            @p.step("gpsimd", reads=[r_xo[bi]], dma=True)
            def _(e, dd, t=t, bi=bi):
                dd(out=xout[t * 128:(t + 1) * 128, :], in_=xo[:, bi, :])
        es2.close()
        p.drain()
    es.close()
    return nc


def _halo_slice(x_full, c):
    b, sl = c // 4, c % 4
    start, end = sl * TC, (sl + 1) * TC
    out = np.zeros((TC + 256, D), np.float32)
    lo, hi = max(0, start - 128), min(T, end + 128)
    out[lo - (start - 128):hi - (start - 128)] = x_full[b, lo:hi]
    return out


def _l0attn_consts(c):
    b, sl = c // 4, c % 4
    start = sl * TC
    pos = np.arange(start - 128, start + TC + 128)
    inv = (1.0 / 10000.0 ** (np.arange(0, 64, 2, dtype=np.float32) / 64)).astype(np.float32)
    pv = np.where((pos >= 0) & (pos < T), pos, 0).astype(np.float32)
    fr = np.outer(pv, inv).astype(np.float32)
    cos, sin = np.cos(fr).astype(np.float32), np.sin(fr).astype(np.float32)
    qb = np.arange(16)[:, None, None]
    qi = np.arange(128)[None, :, None]
    ki = np.arange(384)[None, None, :]
    gq = start + qb * 128 + qi
    gk = start - 128 + qb * 128 + ki
    keep = (np.abs(gk - gq) <= 128) & (gk >= 0) & (gk < T)
    msk = keep.astype(BF).reshape(16 * 128, 384)
    return cos, sin, msk


def run_l0attn(x_full, wqkv, wout, bout):
    nc = _get("l0attn", build_l0attn)
    in_maps = []
    for c in range(NCORES):
        cos, sin, msk = _l0attn_consts(c)
        in_maps.append({"xin": _halo_slice(x_full, c), "wqkv": wqkv, "wout": wout,
                        "bout": bout.reshape(1, D), "cosd": cos, "sind": sin,
                        "mskd": msk, "idb": _EYE_BF})
    res = run_bass_kernel_spmd(nc, in_maps, core_ids=list(range(NCORES)))
    return np.stack([r["xout"] for r in res.results]).reshape(2, 8192, 1024)





def build_l1attn_pre():
    nc = bass.Bass("TRN2", target_bir_lowering=False)
    xin = nc.dram_tensor("xin", [TC, D], F32, kind="ExternalInput").ap()
    wq_d = nc.dram_tensor("wq", [D, D], F32, kind="ExternalInput").ap()
    wk_d = nc.dram_tensor("wk", [D, D], F32, kind="ExternalInput").ap()
    wv_d = nc.dram_tensor("wv", [D, D], F32, kind="ExternalInput").ap()
    gb_d = nc.dram_tensor("gb", [128, 4 * 64], F32, kind="ExternalInput").ap()
    idb_d = nc.dram_tensor("idb", [128, 128], BF16, kind="ExternalInput").ap()
    qT_o = nc.dram_tensor("qT_o", [8, 128, TC], BF16, kind="ExternalOutput").ap()
    kT_o = nc.dram_tensor("kT_o", [8, 128, TC], BF16, kind="ExternalOutput").ap()
    ktm_o = nc.dram_tensor("ktm_o", [TC, D], BF16, kind="ExternalOutput").ap()
    vtm_o = nc.dram_tensor("vtm_o", [TC, D], BF16, kind="ExternalOutput").ap()

    es = ExitStack()
    sb = lambda *a: es.enter_context(nc.sbuf_tensor(*a))
    ps = lambda *a: es.enter_context(nc.psum_tensor(*a))
    idb = sb("idb_sb", [128, 128], BF16)
    gb = sb("gb_sb", [128, 4 * 64], F32)
    wln = sb("wln", [128, 3, 8, 1024], BF16)
    wT = sb("wT", [128, 3, 8, 1024], BF16)
    xt = sb("xt", [128, 2, D], F32)
    nbf = sb("nbf", [128, 2, D], BF16)
    ntS = sb("ntS", [128, 2, D], BF16)
    qtm = sb("qtm", [128, 2, D], BF16)
    ktm = sb("ktm", [128, 2, D], BF16)
    vtm = sb("vtm", [128, 2, D], BF16)
    sca = sb("sca", [128, 2048], F32)
    hst = sb("hst", [128, 8, 16], F32)
    sT = sb("sT", [128, 8, 128], BF16)
    nsm = sb("nsm3", [128, 16], F32)
    tpA = ps("tpA", [128, 128], BF16)
    tpB = ps("tpB", [128, 128], BF16)
    mmA = ps("mmA", [128, 512], F32)
    mmB = ps("mmB", [128, 512], F32)
    with nc.Block() as blk:
        p = Prog(nc, blk)
        st = dict(s1=nsm[:, 0:1], s2=nsm[:, 1:2], mu=nsm[:, 2:3], m2=nsm[:, 3:4],
                  t=nsm[:, 4:5], var=nsm[:, 5:6], vpe=nsm[:, 6:7], std=nsm[:, 7:8],
                  rstd=nsm[:, 8:9], sq=sca)
        hs = dict(s1=hst[:, 0, :], s2=hst[:, 1, :], mu=hst[:, 2, :], m2=hst[:, 3, :],
                  t=hst[:, 4, :], var=hst[:, 5, :], vpe=hst[:, 6, :], std=hst[:, 7, :])
        r_const = Res()
        r_wln, r_wT = Res(), Res()
        r_xt = [Res(), Res()]
        r_nbf = [Res(), Res()]
        r_ntS = [Res(), Res()]
        r_qtm = [Res(), Res()]
        r_ktm = [Res(), Res()]
        r_vtm = [Res(), Res()]
        r_sca, r_hst, r_nsm, r_sT = Res(), Res(), Res(), Res()
        r_tp = [Res(), Res()]
        r_mm = [Res(), Res()]
        tp, mm = [tpA, tpB], [mmA, mmB]

        @p.step("sync", writes=[r_const], dma=True)
        def _(e, dd):
            dd(out=idb[:], in_=idb_d[:, :])
            dd(out=gb[:], in_=gb_d[:, :])

        @p.step("gpsimd", writes=[r_wln], dma=True)
        def _(e, dd):
            dd(out=wln[:, 0], in_=wq_d.rearrange("(c p) o -> p c o", p=128))
            dd(out=wln[:, 1], in_=wk_d.rearrange("(c p) o -> p c o", p=128))
            dd(out=wln[:, 2], in_=wv_d.rearrange("(c p) o -> p c o", p=128))

        for w in range(3):
            for oc in range(8):
                for ic in range(8):
                    ti = ic % 2
                    @p.step("tensor", reads=[r_wln, r_const], writes=[r_tp[ti]])
                    def _(e, w=w, oc=oc, ic=ic, ti=ti):
                        return e.transpose(tp[ti][:], wln[:, w, oc, ic * 128:(ic + 1) * 128], idb[:])
                    @p.step("vector", reads=[r_tp[ti]], writes=[r_wT])
                    def _(e, w=w, oc=oc, ic=ic, ti=ti):
                        return e.tensor_copy(wT[:, w, ic, oc * 128:(oc + 1) * 128], tp[ti][:])

        def headln(e, src_ap, dst_ap, g0):
            ch = p.chain(e)
            x3 = src_ap.rearrange("p (h x) -> p h x", x=64)
            e.tensor_reduce(hs['s1'], x3, AX.X, ALU.add)
            e.tensor_tensor(sca[:, 0:1024], src_ap, src_ap, ALU.mult)
            ch(e.tensor_reduce(hs['s2'], sca[:, 0:1024].rearrange("p (h x) -> p h x", x=64),
                               AX.X, ALU.add))
            e.tensor_scalar_mul(hs['mu'], hs['s1'], 1.0 / 64)
            ch(e.tensor_scalar_mul(hs['m2'], hs['s2'], 1.0 / 64))
            ch(e.tensor_tensor(hs['t'], hs['mu'], hs['mu'], ALU.mult))
            ch(e.tensor_tensor(hs['var'], hs['m2'], hs['t'], ALU.subtract))
            return e.tensor_scalar_add(hs['vpe'], hs['var'], 1e-5)

        def headln_apply(e, src_ap, dst_ap, g0):
            ch = p.chain(e)
            ch(e.reciprocal(hs['s1'], hs['std']))
            x3 = src_ap.rearrange("p (h x) -> p h x", x=64)
            a3 = sca[:, 0:1024].rearrange("p (h x) -> p h x", x=64)
            mu_b = hs['mu'].unsqueeze(2).broadcast_to([128, 16, 64])
            rs_b = hs['s1'].unsqueeze(2).broadcast_to([128, 16, 64])
            gam = gb[:, g0 * 64:(g0 + 1) * 64].unsqueeze(1).broadcast_to([128, 16, 64])
            bet = gb[:, (g0 + 1) * 64:(g0 + 2) * 64].unsqueeze(1).broadcast_to([128, 16, 64])
            e.tensor_tensor(a3, x3, mu_b, ALU.subtract)
            e.tensor_tensor(a3, a3, rs_b, ALU.mult)
            e.tensor_tensor(a3, a3, gam, ALU.mult)
            d3 = dst_ap.rearrange("p (h x) -> p h x", x=64)
            return e.tensor_tensor(d3, a3, bet, ALU.add)

        for t in range(NT_L1 := TC // 128):
            bi = t % 2
            @p.step("sync", writes=[r_xt[bi]], dma=True)
            def _(e, dd, t=t, bi=bi):
                dd(out=xt[:, bi, :], in_=xin[t * 128:(t + 1) * 128, :])
            @p.step("vector", reads=[r_xt[bi]], writes=[r_nsm, r_sca])
            def _(e, bi=bi):
                return emit_ln_stats(e, xt[:, bi, :], st, 1024, p.chain(e))
            @p.step("scalar", reads=[r_nsm], writes=[r_nsm])
            def _(e):
                return e.activation(out=st['std'], in_=st['vpe'], func=AF.Sqrt)
            @p.step("vector", reads=[r_xt[bi], r_nsm], writes=[r_nbf[bi], r_nsm])
            def _(e, bi=bi):
                p.chain(e)(e.reciprocal(st['rstd'], st['std']))
                return e.tensor_scalar(nbf[:, bi, :], xt[:, bi, :],
                                       st['mu'], st['rstd'], ALU.subtract, ALU.mult)
            for d in range(8):
                ti = d % 2
                @p.step("tensor", reads=[r_nbf[bi], r_const], writes=[r_tp[ti]])
                def _(e, bi=bi, d=d, ti=ti):
                    return e.transpose(tp[ti][:], nbf[:, bi, d * 128:(d + 1) * 128], idb[:])
                @p.step("vector", reads=[r_tp[ti]], writes=[r_ntS[bi]])
                def _(e, bi=bi, d=d, ti=ti):
                    return e.tensor_copy(ntS[:, bi, d * 128:(d + 1) * 128], tp[ti][:])
            for (w, dstb, dres) in ((0, qtm, r_qtm[bi]), (1, ktm, r_ktm[bi]), (2, vtm, r_vtm[bi])):
                for oi in range(2):
                    mi = oi % 2
                    @p.step("tensor", reads=[r_ntS[bi], r_wT], writes=[r_mm[mi]])
                    def _(e, bi=bi, w=w, oi=oi, mi=mi):
                        for d in range(8):
                            last = e.matmul(mm[mi][:], ntS[:, bi, d * 128:(d + 1) * 128],
                                            wT[:, w, d, oi * 512:(oi + 1) * 512],
                                            start=(d == 0), stop=(d == 7))
                        return last
                    @p.step("vector", reads=[r_mm[mi]], writes=[dres])
                    def _(e, bi=bi, oi=oi, mi=mi, dstb=dstb):
                        return e.tensor_copy(dstb[:, bi, oi * 512:(oi + 1) * 512], mm[mi][:])
            for (buf, rres, g0) in ((qtm, r_qtm[bi], 0), (ktm, r_ktm[bi], 2)):
                @p.step("vector", reads=[rres], writes=[r_hst, r_sca])
                def _(e, bi=bi, buf=buf, g0=g0):
                    return headln(e, buf[:, bi, :], None, g0)
                @p.step("scalar", reads=[r_hst], writes=[r_hst])
                def _(e):
                    return e.activation(out=hs['std'], in_=hs['vpe'], func=AF.Sqrt)
                @p.step("vector", reads=[rres, r_hst, r_const], writes=[rres, r_hst, r_sca])
                def _(e, bi=bi, buf=buf, g0=g0):
                    return headln_apply(e, buf[:, bi, :], buf[:, bi, :], g0)
            @p.step("gpsimd", reads=[r_ktm[bi], r_vtm[bi]], dma=True)
            def _(e, dd, t=t, bi=bi):
                dd(out=ktm_o[t * 128:(t + 1) * 128, :], in_=ktm[:, bi, :])
                dd(out=vtm_o[t * 128:(t + 1) * 128, :], in_=vtm[:, bi, :])
            for (buf, rres, dram) in ((qtm, r_qtm[bi], qT_o), (ktm, r_ktm[bi], kT_o)):
                for d in range(8):
                    ti = d % 2
                    @p.step("tensor", reads=[rres, r_const], writes=[r_tp[ti]])
                    def _(e, bi=bi, d=d, ti=ti, buf=buf):
                        return e.transpose(tp[ti][:], buf[:, bi, d * 128:(d + 1) * 128], idb[:])
                    @p.step("vector", reads=[r_tp[ti]], writes=[r_sT])
                    def _(e, d=d, ti=ti):
                        return e.tensor_copy(sT[:, d, :], tp[ti][:])
                @p.step("gpsimd", reads=[r_sT], dma=True)
                def _(e, dd, t=t, dram=dram):
                    dd(out=dram[:, :, t * 128:(t + 1) * 128].rearrange("c p q -> p c q"),
                       in_=sT[:])
        p.drain()
    es.close()
    return nc


def build_l1attn_scan():
    nc = bass.Bass("TRN2", target_bir_lowering=False)
    NP = 4
    qT_d = nc.dram_tensor("qT", [NP, 64, T], BF16, kind="ExternalInput").ap()
    kT_d = nc.dram_tensor("kT", [NP, 64, T], BF16, kind="ExternalInput").ap()
    ktm_d = nc.dram_tensor("ktm", [NP, T, 64], BF16, kind="ExternalInput").ap()
    vtm_d = nc.dram_tensor("vtm", [NP, T, 64], BF16, kind="ExternalInput").ap()
    tri_d = nc.dram_tensor("tri", [128, 128], BF16, kind="ExternalInput").ap()
    oT_o = nc.dram_tensor("oT_o", [NP, 64, T], BF16, kind="ExternalOutput").ap()

    NCH = T // 128
    es = ExitStack()
    sb = lambda *a: es.enter_context(nc.sbuf_tensor(*a))
    ps = lambda *a: es.enter_context(nc.psum_tensor(*a))
    tri = sb("tri_sb", [128, 128], BF16)
    qTs = sb("qTs", [64, 2, T], BF16)
    kTs = sb("kTs", [64, 2, T], BF16)
    ktms = sb("ktms", [128, 2, NCH, 64], BF16)
    vtms = sb("vtms", [128, 2, NCH, 64], BF16)
    Sf = sb("Sf", [64, 64], F32)
    Sb = sb("Sb", [64, 64], BF16)
    asb = sb("asb", [128, 2, 128], BF16)
    ost = sb("ost", [64, 2, 128], BF16)
    aps_ = ps("aps", [128, 2, 128], F32)
    ops2 = ps("ops2", [64, 2, 128], F32)
    tps = ps("tps", [64, 2, 64], F32)
    with nc.Block() as blk:
        p = Prog(nc, blk)
        r_const = Res()
        r_in = [Res(), Res()]
        r_S = Res()
        r_asb = [Res(), Res()]
        r_ost = [Res(), Res()]
        r_aps = [Res(), Res()]
        r_ops = [Res(), Res()]
        r_tps = [Res(), Res()]

        @p.step("sync", writes=[r_const], dma=True)
        def _(e, dd):
            dd(out=tri[:], in_=tri_d[:, :])

        for j in range(NP):
            bj = j % 2
            @p.step("sync", writes=[r_in[bj]], dma=True)
            def _(e, dd, j=j, bj=bj):
                dd(out=qTs[:, bj, :], in_=qT_d[j])
                dd(out=kTs[:, bj, :], in_=kT_d[j])
                dd(out=ktms[:, bj], in_=ktm_d[j].rearrange("(c p) x -> p c x", p=128))
                dd(out=vtms[:, bj], in_=vtm_d[j].rearrange("(c p) x -> p c x", p=128))
            @p.step("vector", writes=[r_S])
            def _(e):
                e.memset(Sf[:], 0.0)
                return e.memset(Sb[:], 0.0)
            for c in range(NCH):
                ci = c % 2
                @p.step("tensor", reads=[r_in[bj]], writes=[r_aps[ci]])
                def _(e, bj=bj, c=c, ci=ci):
                    return e.matmul(aps_[:, ci, :], kTs[:, bj, c * 128:(c + 1) * 128],
                                    qTs[:, bj, c * 128:(c + 1) * 128], start=True, stop=True)
                @p.step("vector", reads=[r_aps[ci], r_const], writes=[r_asb[ci]])
                def _(e, ci=ci):
                    return e.tensor_tensor(asb[:, ci, :], aps_[:, ci, :], tri[:], ALU.mult)
                @p.step("tensor", reads=[r_asb[ci], r_in[bj], r_S], writes=[r_ops[ci]])
                def _(e, bj=bj, c=c, ci=ci):
                    e.matmul(ops2[:, ci, :], vtms[:, bj, c, :], asb[:, ci, :],
                             start=True, stop=False)
                    return e.matmul(ops2[:, ci, :], Sb[:, :], qTs[:, bj, c * 128:(c + 1) * 128],
                                    start=False, stop=True)
                @p.step("tensor", reads=[r_in[bj]], writes=[r_tps[ci]])
                def _(e, bj=bj, c=c, ci=ci):
                    return e.matmul(tps[:, ci, :], ktms[:, bj, c, :], vtms[:, bj, c, :],
                                    start=True, stop=True)
                @p.step("vector", reads=[r_ops[ci]], writes=[r_ost[ci]])
                def _(e, ci=ci):
                    return e.tensor_copy(ost[:, ci, :], ops2[:, ci, :])
                @p.step("gpsimd", reads=[r_ost[ci]], dma=True)
                def _(e, dd, j=j, c=c, ci=ci):
                    dd(out=oT_o[j][:, c * 128:(c + 1) * 128], in_=ost[:, ci, :])
                @p.step("vector", reads=[r_tps[ci], r_S], writes=[r_S])
                def _(e, ci=ci):
                    ch = p.chain(e)
                    ch(e.tensor_tensor(Sf[:], Sf[:], tps[:, ci, :], ALU.add))
                    return e.tensor_copy(Sb[:], Sf[:])
        p.drain()
    es.close()
    return nc


def build_l1attn_post():
    nc = bass.Bass("TRN2", target_bir_lowering=False)
    xin = nc.dram_tensor("xin", [TC, D], F32, kind="ExternalInput").ap()
    oT_d = nc.dram_tensor("oT", [8, 128, TC], BF16, kind="ExternalInput").ap()
    wo_d = nc.dram_tensor("wo", [D, D], F32, kind="ExternalInput").ap()
    idb_d = nc.dram_tensor("idb", [128, 128], BF16, kind="ExternalInput").ap()
    xout = nc.dram_tensor("xout", [TC, D], F32, kind="ExternalOutput").ap()

    es = ExitStack()
    sb = lambda *a: es.enter_context(nc.sbuf_tensor(*a))
    ps = lambda *a: es.enter_context(nc.psum_tensor(*a))
    idb = sb("idb_sb", [128, 128], BF16)
    wln = sb("wln", [128, 8, 1024], BF16)
    wT = sb("wT", [128, 8, 1024], BF16)
    oTs = sb("oTs", [128, 8, TC], BF16)
    xt = sb("xt", [128, 2, D], F32)
    xo = sb("xo", [128, 2, D], F32)
    tpA = ps("tpA", [128, 128], BF16)
    tpB = ps("tpB", [128, 128], BF16)
    mmA = ps("mmA", [128, 512], F32)
    mmB = ps("mmB", [128, 512], F32)
    with nc.Block() as blk:
        p = Prog(nc, blk)
        r_const, r_wln, r_wT, r_oTs = Res(), Res(), Res(), Res()
        r_xt = [Res(), Res()]
        r_xo = [Res(), Res()]
        r_tp = [Res(), Res()]
        r_mm = [Res(), Res()]
        tp, mm = [tpA, tpB], [mmA, mmB]

        @p.step("sync", writes=[r_const, r_oTs], dma=True)
        def _(e, dd):
            dd(out=idb[:], in_=idb_d[:, :])
            dd(out=oTs[:], in_=oT_d.rearrange("c p q -> p c q"))

        @p.step("gpsimd", writes=[r_wln], dma=True)
        def _(e, dd):
            dd(out=wln[:], in_=wo_d.rearrange("(c p) o -> p c o", p=128))

        for oc in range(8):
            for ic in range(8):
                ti = ic % 2
                @p.step("tensor", reads=[r_wln, r_const], writes=[r_tp[ti]])
                def _(e, oc=oc, ic=ic, ti=ti):
                    return e.transpose(tp[ti][:], wln[:, oc, ic * 128:(ic + 1) * 128], idb[:])
                @p.step("vector", reads=[r_tp[ti]], writes=[r_wT])
                def _(e, oc=oc, ic=ic, ti=ti):
                    return e.tensor_copy(wT[:, ic, oc * 128:(oc + 1) * 128], tp[ti][:])

        for t in range(TC // 128):
            bi = t % 2
            @p.step("sync", writes=[r_xt[bi]], dma=True)
            def _(e, dd, t=t, bi=bi):
                dd(out=xt[:, bi, :], in_=xin[t * 128:(t + 1) * 128, :])
            for oi in range(2):
                mi = oi % 2
                @p.step("tensor", reads=[r_oTs, r_wT], writes=[r_mm[mi]])
                def _(e, t=t, oi=oi, mi=mi):
                    for d in range(8):
                        last = e.matmul(mm[mi][:], oTs[:, d, t * 128:(t + 1) * 128],
                                        wT[:, d, oi * 512:(oi + 1) * 512],
                                        start=(d == 0), stop=(d == 7))
                    return last
                @p.step("vector", reads=[r_mm[mi], r_xt[bi]], writes=[r_xo[bi]])
                def _(e, oi=oi, mi=mi, bi=bi):
                    return e.tensor_tensor(xo[:, bi, oi * 512:(oi + 1) * 512], mm[mi][:],
                                           xt[:, bi, oi * 512:(oi + 1) * 512], ALU.add)
            @p.step("gpsimd", reads=[r_xo[bi]], dma=True)
            def _(e, dd, t=t, bi=bi):
                dd(out=xout[t * 128:(t + 1) * 128, :], in_=xo[:, bi, :])
        p.drain()
    es.close()
    return nc


def run_l1attn(x_full, Wq, Wk, Wv, Wo, qg, qb_, kg, kb_):
    nc = _get("l1attn_pre", build_l1attn_pre)
    gbrep = np.concatenate([np.broadcast_to(v.reshape(1, 64), (128, 64)) for v in
                            (qg, qb_, kg, kb_)], axis=1).astype(np.float32)
    xs = x_full.reshape(8, TC, D)
    im = [{"xin": np.ascontiguousarray(xs[c]), "wq": Wq, "wk": Wk, "wv": Wv,
           "gb": gbrep, "idb": _EYE_BF} for c in range(NCORES)]
    r1 = run_bass_kernel_spmd(nc, im, core_ids=list(range(NCORES))).results
    qT = np.zeros((2, 16, 64, T), BF)
    kT = np.zeros((2, 16, 64, T), BF)
    ktm = np.zeros((2, T, 16, 64), BF)
    vtm = np.zeros((2, T, 16, 64), BF)
    for c in range(NCORES):
        b, sl = c // 4, c % 4
        cols = slice(sl * TC, (sl + 1) * TC)
        for h in range(16):
            qT[b, h, :, cols] = r1[c]["qT_o"][h // 2, (h % 2) * 64:(h % 2) * 64 + 64, :]
            kT[b, h, :, cols] = r1[c]["kT_o"][h // 2, (h % 2) * 64:(h % 2) * 64 + 64, :]
        ktm[b, cols.start:cols.stop] = r1[c]["ktm_o"].reshape(TC, 16, 64)
        vtm[b, cols.start:cols.stop] = r1[c]["vtm_o"].reshape(TC, 16, 64)
    nc2 = _get("l1attn_scan", build_l1attn_scan)
    tri = np.triu(np.ones((128, 128), np.float32)).astype(BF)
    im2 = []
    for c in range(NCORES):
        prs = [4 * c + j for j in range(4)]
        im2.append({
            "qT": np.stack([qT[pp // 16, pp % 16] for pp in prs]),
            "kT": np.stack([kT[pp // 16, pp % 16] for pp in prs]),
            "ktm": np.stack([np.ascontiguousarray(ktm[pp // 16, :, pp % 16]) for pp in prs]),
            "vtm": np.stack([np.ascontiguousarray(vtm[pp // 16, :, pp % 16]) for pp in prs]),
            "tri": tri,
        })
    r2 = run_bass_kernel_spmd(nc2, im2, core_ids=list(range(NCORES))).results
    oT = np.zeros((2, 16, 64, T), BF)
    for c in range(NCORES):
        for j in range(4):
            pp = 4 * c + j
            oT[pp // 16, pp % 16] = r2[c]["oT_o"][j]
    nc3 = _get("l1attn_post", build_l1attn_post)
    im3 = []
    for c in range(NCORES):
        b, sl = c // 4, c % 4
        cols = slice(sl * TC, (sl + 1) * TC)
        oTi = np.zeros((8, 128, TC), BF)
        for h in range(16):
            oTi[h // 2, (h % 2) * 64:(h % 2) * 64 + 64, :] = oT[b, h][:, cols]
        im3.append({"xin": np.ascontiguousarray(xs[c]), "oT": oTi, "wo": Wo, "idb": _EYE_BF})
    r3 = run_bass_kernel_spmd(nc3, im3, core_ids=list(range(NCORES))).results
    return np.stack([r["xout"] for r in r3]).reshape(2, 8192, 1024)





def build_l1ffn_pre(debug=False):
    nc = bass.Bass("TRN2", target_bir_lowering=False)
    xin = nc.dram_tensor("xin", [TC, D], F32, kind="ExternalInput").ap()
    qw_d = nc.dram_tensor("qw", [D, D], F32, kind="ExternalInput").ap()
    kw_d = nc.dram_tensor("kw", [D, D], F32, kind="ExternalInput").ap()
    selw_d = nc.dram_tensor("selw", [128, D], F32, kind="ExternalInput").ap()
    vw_d = nc.dram_tensor("vw", [64, D, 128], F32, kind="ExternalInput").ap()
    rsc_d = nc.dram_tensor("rsc", [128, 1], F32, kind="ExternalInput").ap()
    idb_d = nc.dram_tensor("idb", [128, 128], BF16, kind="ExternalInput").ap()
    qT_o = nc.dram_tensor("qT_o", [8, 128, TC], BF16, kind="ExternalOutput").ap()
    kT_o = nc.dram_tensor("kT_o", [8, 128, TC], BF16, kind="ExternalOutput").ap()
    V_o = nc.dram_tensor("V_o", [TC, 8, 128], BF16, kind="ExternalOutput").ap()
    go_o = nc.dram_tensor("go_o", [TC, 64], F32, kind="ExternalOutput").ap()
    if debug:
        dbg_probs = nc.dram_tensor("dbg_probs", [128, 16 * 128], F32, kind="ExternalOutput").ap()
        dbg_rank = nc.dram_tensor("dbg_rank", [128, 16 * 128], F32, kind="ExternalOutput").ap()
        dbg_gates = nc.dram_tensor("dbg_gates", [128, 16 * 128], F32, kind="ExternalOutput").ap()

    NT16 = TC // 128
    es = ExitStack()
    sb = lambda *a: es.enter_context(nc.sbuf_tensor(*a))
    ps = lambda *a: es.enter_context(nc.psum_tensor(*a))
    idb = sb("idb_sb", [128, 128], BF16)
    rsc = sb("rsc_sb", [128, 1], F32)
    selT = sb("selT", [128, 8, 128], BF16)
    selw = sb("selw_sb", [128, 1024], BF16)
    nT = sb("nT", [128, 8, TC], BF16)
    qst = sb("qst", [128, 2, 512], BF16)
    probs = sb("probs", [128, NT16, 128], F32)
    rank = sb("rank", [128, NT16, 128], F32)
    gates = sb("gates", [128, NT16, 16, 8], F32)
    Vb = sb("Vb", [128, NT16, 8, 128], BF16)
    xt = sb("xt", [128, 2, D], F32)
    nbf = sb("nbf", [128, 2, D], BF16)
    wvq = sb("wvq", [128, 2, 8, 512], BF16)
    nsm = sb("nsm4", [128, 16], F32)
    sca = sb("sca", [128, 2304], F32)
    tpA = ps("tpA", [128, 128], BF16)
    tpB = ps("tpB", [128, 128], BF16)
    mmA = ps("mmA", [128, 512], F32)
    mmB = ps("mmB", [128, 512], F32)
    lgp = ps("lgp", [128, 128], F32)
    with nc.Block() as blk:
        p = Prog(nc, blk)
        st = dict(s1=nsm[:, 0:1], s2=nsm[:, 1:2], mu=nsm[:, 2:3], m2=nsm[:, 3:4],
                  t=nsm[:, 4:5], var=nsm[:, 5:6], vpe=nsm[:, 6:7], std=nsm[:, 7:8],
                  rstd=nsm[:, 8:9], sq=sca)
        r_const, r_selw, r_selT = Res(), Res(), Res()
        r_nT = Res()
        r_probs, r_rank, r_gates, r_Vb = Res(), Res(), Res(), Res()
        r_xt = [Res(), Res()]
        r_nbf = [Res(), Res()]
        r_wvq = [Res(), Res()]
        r_nsm, r_sca = Res(), Res()
        r_tp = [Res(), Res()]
        r_mm = [Res(), Res()]
        r_lgp = Res()
        tp, mm = [tpA, tpB], [mmA, mmB]

        @p.step("sync", writes=[r_const], dma=True)
        def _(e, dd):
            dd(out=idb[:], in_=idb_d[:, :])
            dd(out=rsc[:], in_=rsc_d[:, :])

        @p.step("gpsimd", writes=[r_selw], dma=True)
        def _(e, dd):
            dd(out=selw[:], in_=selw_d[:, :])

        for ic in range(8):
            ti = ic % 2
            @p.step("tensor", reads=[r_selw, r_const], writes=[r_tp[ti]])
            def _(e, ic=ic, ti=ti):
                return e.transpose(tp[ti][:], selw[:, ic * 128:(ic + 1) * 128], idb[:])
            @p.step("vector", reads=[r_tp[ti]], writes=[r_selT])
            def _(e, ic=ic, ti=ti):
                return e.tensor_copy(selT[:, ic, :], tp[ti][:])

        es1 = ExitStack()
        wqkT = es1.enter_context(nc.sbuf_tensor("wqkT", [128, 2, 8, 1024], BF16))
        wln = es1.enter_context(nc.sbuf_tensor("wln1", [128, 8, 1024], BF16))
        r_wqkT, r_wln = Res(), Res()
        for w, wd in ((0, qw_d), (1, kw_d)):
            @p.step("gpsimd", writes=[r_wln], dma=True)
            def _(e, dd, wd=wd):
                dd(out=wln[:], in_=wd.rearrange("(c p) o -> p c o", p=128))
            for oc in range(8):
                for ic in range(8):
                    ti = ic % 2
                    @p.step("tensor", reads=[r_wln, r_const], writes=[r_tp[ti]])
                    def _(e, oc=oc, ic=ic, ti=ti):
                        return e.transpose(tp[ti][:], wln[:, oc, ic * 128:(ic + 1) * 128], idb[:])
                    @p.step("vector", reads=[r_tp[ti]], writes=[r_wqkT])
                    def _(e, w=w, oc=oc, ic=ic, ti=ti):
                        return e.tensor_copy(wqkT[:, w, ic, oc * 128:(oc + 1) * 128], tp[ti][:])

        for t in range(NT16):
            bi = t % 2
            @p.step("sync", writes=[r_xt[bi]], dma=True)
            def _(e, dd, t=t, bi=bi):
                dd(out=xt[:, bi, :], in_=xin[t * 128:(t + 1) * 128, :])
            @p.step("vector", reads=[r_xt[bi]], writes=[r_nsm, r_sca])
            def _(e, bi=bi):
                return emit_ln_stats(e, xt[:, bi, :], st, 1024, p.chain(e))
            @p.step("scalar", reads=[r_nsm], writes=[r_nsm])
            def _(e):
                return e.activation(out=st['std'], in_=st['vpe'], func=AF.Sqrt)
            @p.step("vector", reads=[r_xt[bi], r_nsm], writes=[r_nbf[bi], r_nsm])
            def _(e, bi=bi):
                p.chain(e)(e.reciprocal(st['rstd'], st['std']))
                return e.tensor_scalar(nbf[:, bi, :], xt[:, bi, :],
                                       st['mu'], st['rstd'], ALU.subtract, ALU.mult)
            for d in range(8):
                ti = d % 2
                @p.step("tensor", reads=[r_nbf[bi], r_const], writes=[r_tp[ti]])
                def _(e, bi=bi, d=d, ti=ti):
                    return e.transpose(tp[ti][:], nbf[:, bi, d * 128:(d + 1) * 128], idb[:])
                @p.step("vector", reads=[r_tp[ti]], writes=[r_nT])
                def _(e, d=d, ti=ti, t=t):
                    return e.tensor_copy(nT[:, d, t * 128:(t + 1) * 128], tp[ti][:])
            @p.step("tensor", reads=[r_nT, r_selT], writes=[r_lgp])
            def _(e, t=t):
                for d in range(8):
                    last = e.matmul(lgp[:], nT[:, d, t * 128:(t + 1) * 128],
                                    selT[:, d, :], start=(d == 0), stop=(d == 7))
                return last
            @p.step("scalar", reads=[r_lgp], writes=[r_probs])
            def _(e, t=t):
                return e.activation(out=probs[:, t, :], in_=lgp[:], func=AF.Sigmoid)

        r_qst = [Res(), Res()]
        for w, dram in ((0, qT_o), (1, kT_o)):
            for hp in range(8):
                for tq in range(4):
                    mi = (hp * 4 + tq) % 2
                    @p.step("tensor", reads=[r_nT, r_wqkT], writes=[r_mm[mi]])
                    def _(e, w=w, hp=hp, tq=tq, mi=mi):
                        for d in range(8):
                            last = e.matmul(mm[mi][:], wqkT[:, w, d, hp * 128:(hp + 1) * 128],
                                            nT[:, d, tq * 512:(tq + 1) * 512],
                                            start=(d == 0), stop=(d == 7))
                        return last
                    @p.step("vector", reads=[r_mm[mi]], writes=[r_qst[mi]])
                    def _(e, mi=mi):
                        return e.tensor_copy(qst[:, mi, :], mm[mi][:])
                    @p.step("gpsimd", reads=[r_qst[mi]], dma=True)
                    def _(e, dd, hp=hp, tq=tq, mi=mi, dram=dram):
                        dd(out=dram[hp][:, tq * 512:(tq + 1) * 512], in_=qst[:, mi, :])
        es1.close()

        @p.step("vector", reads=[r_probs, r_const], writes=[r_rank, r_gates, r_sca])
        def _(e):
            ch = p.chain(e)
            p3 = probs[:].rearrange("p t (g e) -> p (t g) e", e=8)
            r3 = rank[:].rearrange("p t (g e) -> p (t g) e", e=8)
            e.memset(rank[:], 0.0)
            NG = NT16 * 16
            t1 = sca[:, 0:NG * 8].rearrange("p (g e) -> p g e", e=8)
            for dlt in range(1, 8):
                wdt = 8 - dlt
                a = p3[:, :, 0:wdt]
                b = p3[:, :, dlt:8]
                ch(e.tensor_tensor(t1[:, :, 0:wdt], b, a, ALU.is_gt))
                ch(e.tensor_tensor(r3[:, :, 0:wdt], r3[:, :, 0:wdt], t1[:, :, 0:wdt], ALU.add))
                ch(e.tensor_tensor(t1[:, :, 0:wdt], a, b, ALU.is_ge))
                ch(e.tensor_tensor(r3[:, :, dlt:8], r3[:, :, dlt:8], t1[:, :, 0:wdt], ALU.add))
            ch(e.tensor_single_scalar(r3, r3, 2.0, ALU.is_lt))
            ch(e.tensor_tensor(t1, p3, r3, ALU.mult))
            s_ = sca[:, NG * 8:NG * 9]
            ch(e.tensor_reduce(s_.rearrange("p (g x) -> p g x", x=1), t1, AX.X, ALU.add))
            ch(e.tensor_scalar_max(s_, s_, 1e-9))
            ch(e.reciprocal(s_, s_))
            ch(e.tensor_scalar_mul(s_, s_, rsc[:, 0:1]))
            sb_ = s_.rearrange("p (g x) -> p g x", x=1).broadcast_to([128, NG, 8])
            g4 = gates[:].rearrange("p t g e -> p (t g) e")
            return e.tensor_tensor(g4, t1, sb_, ALU.mult)

        @p.step("gpsimd", reads=[r_gates], dma=True)
        def _(e, dd):
            dd(out=go_o.rearrange("(t p) (h e) -> p t h e", p=128, e=8),
               in_=gates[:, :, 8:16, :])
        if debug:
            @p.step("gpsimd", reads=[r_probs, r_rank, r_gates], dma=True)
            def _(e, dd):
                dd(out=dbg_probs.rearrange("p (t x) -> p t x", x=128), in_=probs[:])
                dd(out=dbg_rank.rearrange("p (t x) -> p t x", x=128), in_=rank[:])
                dd(out=dbg_gates.rearrange("p (t h e) -> p t h e", h=16, e=8), in_=gates[:])

        for Q in range(16):
            bq = Q % 2
            @p.step("gpsimd", writes=[r_wvq[bq]], dma=True)
            def _(e, dd, Q=Q, bq=bq):
                for l in range(4):
                    dd(out=wvq[:, bq, :, l * 128:(l + 1) * 128],
                       in_=vw_d[4 * Q + l].rearrange("(c p) x -> p c x", p=128))
            for t in range(NT16):
                mi = t % 2
                @p.step("tensor", reads=[r_nT, r_wvq[bq]], writes=[r_mm[mi]])
                def _(e, Q=Q, t=t, mi=mi, bq=bq):
                    for d in range(8):
                        last = e.matmul(mm[mi][:], nT[:, d, t * 128:(t + 1) * 128],
                                        wvq[:, bq, d, :], start=(d == 0), stop=(d == 7))
                    return last
                @p.step("vector", reads=[r_mm[mi], r_gates], writes=[r_Vb])
                def _(e, Q=Q, t=t, mi=mi):
                    last = None
                    for l in range(4):
                        he = 4 * Q + l
                        h, ei = he // 8, he % 8
                        gcol = gates[:, t, h, ei:ei + 1]
                        dst = Vb[:, t, h, :]
                        if ei == 0:
                            last = e.tensor_scalar_mul(dst, mm[mi][:, l * 128:(l + 1) * 128], gcol)
                        else:
                            last = e.scalar_tensor_tensor(dst, mm[mi][:, l * 128:(l + 1) * 128],
                                                          gcol, dst, ALU.mult, ALU.add)
                    return last

        @p.step("gpsimd", reads=[r_Vb], dma=True)
        def _(e, dd):
            dd(out=V_o.rearrange("(t p) h x -> p t h x", p=128), in_=Vb[:])
        p.drain()
    es.close()
    return nc


def build_l1ffn_flash(debug=False):
    nc = bass.Bass("TRN2", target_bir_lowering=False)
    NP = 2
    SCALE = float(1.0 / np.sqrt(128.0))
    qT_d = nc.dram_tensor("qT", [NP, 128, T], BF16, kind="ExternalInput").ap()
    kT_d = nc.dram_tensor("kT", [NP, 128, T], BF16, kind="ExternalInput").ap()
    V_d = nc.dram_tensor("V", [NP, T, 129], BF16, kind="ExternalInput").ap()
    ctxT_o = nc.dram_tensor("ctxT_o", [NP, 128, T], BF16, kind="ExternalOutput").ap()
    rs_o = nc.dram_tensor("rs_o", [NP, T], F32, kind="ExternalOutput").ap()
    if debug:
        dbg_nrm = nc.dram_tensor("dbg_nrm", [1, 2 * T], F32, kind="ExternalOutput").ap()
        dbg_smax = nc.dram_tensor("dbg_smax", [1, 8], F32, kind="ExternalOutput").ap()
        dbg_bcol = nc.dram_tensor("dbg_bcol", [128, 1], F32, kind="ExternalOutput").ap()

    NKB = T // 128
    es = ExitStack()
    sb = lambda *a: es.enter_context(nc.sbuf_tensor(*a))
    ps = lambda *a: es.enter_context(nc.psum_tensor(*a))
    qs = sb("qs", [128, 2, T], BF16)
    ks = sb("ks", [128, 2, T], BF16)
    vs = sb("vs", [128, 2, NKB, 129], BF16)
    onec = sb("onec", [128, 1], BF16)
    oner = sb("oner", [1, 128], BF16)
    sqb = sb("sqb", [128, 512], BF16)
    nrm = sb("nrm", [1, 2 * T], F32)
    smax = sb("smax", [1, 8], F32)
    bcol = sb("bcol", [128, 1], F32)
    pbt = sb("pbt", [128, 2, 512], BF16)
    cst = sb("cst", [128, 512], BF16)
    rst = sb("rst", [1, 512], F32)
    scp = ps("scp", [128, 2, 512], F32)
    ctxp = ps("ctxp", [128, 512], F32)
    rsp = ps("rsp", [1, 512], F32)
    nrp = ps("nrp", [1, 512], F32)
    mbp = ps("mbp", [128, 1], F32)
    with nc.Block() as blk:
        p = Prog(nc, blk)
        r_const = Res()
        r_in = [Res(), Res()]
        r_sqb, r_nrm, r_smax, r_bcol = Res(), Res(), Res(), Res()
        r_pbt = [Res(), Res()]
        r_cst, r_rst = Res(), Res()
        r_scp = [Res(), Res()]
        r_ctxp, r_rsp, r_nrp, r_mbp = Res(), Res(), Res(), Res()

        @p.step("vector", writes=[r_const])
        def _(e):
            e.memset(onec[:], 1.0)
            return e.memset(oner[:], 1.0)

        for j in range(NP):
            bj = j % 2
            @p.step("sync", writes=[r_in[bj]], dma=True)
            def _(e, dd, j=j, bj=bj):
                dd(out=qs[:, bj, :], in_=qT_d[j])
                dd(out=ks[:, bj, :], in_=kT_d[j])
                dd(out=vs[:, bj], in_=V_d[j].rearrange("(c p) x -> p c x", p=128))
            for w, buf in ((0, qs), (1, ks)):
                for sl in range(16):
                    @p.step("scalar", reads=[r_in[bj]], writes=[r_sqb])
                    def _(e, bj=bj, buf=buf, sl=sl):
                        return e.activation(out=sqb[:], in_=buf[:, bj, sl * 512:(sl + 1) * 512],
                                            func=AF.Square)
                    @p.step("tensor", reads=[r_sqb, r_const], writes=[r_nrp])
                    def _(e):
                        return e.matmul(nrp[:], onec[:], sqb[:], start=True, stop=True)
                    @p.step("vector", reads=[r_nrp], writes=[r_nrm])
                    def _(e, w=w, sl=sl):
                        return e.tensor_copy(nrm[0:1, w * T + sl * 512: w * T + (sl + 1) * 512],
                                             nrp[:])
                @p.step("vector", reads=[r_nrm], writes=[r_smax])
                def _(e, w=w):
                    return e.reduce_max(smax[0:1, w:w + 1], nrm[0:1, w * T:(w + 1) * T], AX.X)
            @p.step("vector", reads=[r_smax], writes=[r_smax])
            def _(e):
                ch = p.chain(e)
                ch(e.tensor_tensor(smax[0:1, 2:3], smax[0:1, 0:1], smax[0:1, 1:2], ALU.mult))
                return e.tensor_scalar_mul(smax[0:1, 3:4], smax[0:1, 2:3], SCALE * SCALE)
            @p.step("scalar", reads=[r_smax], writes=[r_smax])
            def _(e):
                return e.activation(out=smax[0:1, 4:5], in_=smax[0:1, 3:4], func=AF.Sqrt)
            @p.step("vector", reads=[r_smax], writes=[r_smax])
            def _(e):
                return e.tensor_scalar(smax[0:1, 5:6], smax[0:1, 4:5], -1.0, None,
                                       ALU.mult, out_dtype_hint := ALU.bypass)
            @p.step("vector", reads=[r_smax], writes=[r_smax])
            def _(e):
                return e.tensor_copy(smax[0:1, 6:7].bitcast(F32), smax[0:1, 5:6])
            @p.step("vector", reads=[r_smax], writes=[r_sqb])
            def _(e):
                return e.tensor_copy(sqb[0:1, 0:1], smax[0:1, 5:6])
            @p.step("tensor", reads=[r_sqb, r_const], writes=[r_mbp])
            def _(e):
                return e.matmul(mbp[:], oner[:, :], sqb[0:1, 0:1], start=True, stop=True)
            @p.step("vector", reads=[r_mbp], writes=[r_bcol])
            def _(e):
                return e.tensor_copy(bcol[:], mbp[:])
            if debug and j == 0:
                @p.step("gpsimd", reads=[r_nrm, r_smax, r_bcol], dma=True)
                def _(e, dd):
                    dd(out=dbg_nrm[:, :], in_=nrm[:])
                    dd(out=dbg_smax[:, :], in_=smax[:])
                    dd(out=dbg_bcol[:, :], in_=bcol[:])
            for qt in range(16):
                for kb in range(NKB):
                    ci = kb % 2
                    @p.step("tensor", reads=[r_in[bj]], writes=[r_scp[ci]])
                    def _(e, bj=bj, qt=qt, kb=kb, ci=ci):
                        return e.matmul(scp[:, ci, :], ks[:, bj, kb * 128:(kb + 1) * 128],
                                        qs[:, bj, qt * 512:(qt + 1) * 512], start=True, stop=True)
                    @p.step("scalar", reads=[r_scp[ci], r_bcol], writes=[r_pbt[ci]])
                    def _(e, ci=ci):
                        return e.activation(out=pbt[:, ci, :], in_=scp[:, ci, :], func=AF.Exp,
                                            bias=bcol[:, 0:1], scale=SCALE)
                    @p.step("tensor", reads=[r_pbt[ci], r_in[bj]],
                            writes=[r_ctxp] if kb < NKB - 1 else [r_ctxp],
                            )
                    def _(e, bj=bj, kb=kb, ci=ci):
                        return e.matmul(ctxp[:], vs[:, bj, kb, 0:128], pbt[:, ci, :],
                                        start=(kb == 0), stop=(kb == NKB - 1))
                    @p.step("tensor", reads=[r_pbt[ci], r_in[bj]], writes=[r_rsp])
                    def _(e, bj=bj, kb=kb, ci=ci):
                        return e.matmul(rsp[:], vs[:, bj, kb, 128:129], pbt[:, ci, :],
                                        start=(kb == 0), stop=(kb == NKB - 1))
                @p.step("vector", reads=[r_ctxp], writes=[r_cst])
                def _(e):
                    return e.tensor_copy(cst[:], ctxp[:])
                @p.step("vector", reads=[r_rsp], writes=[r_rst])
                def _(e):
                    return e.tensor_copy(rst[:], rsp[:])
                @p.step("gpsimd", reads=[r_cst, r_rst], dma=True)
                def _(e, dd, j=j, qt=qt):
                    dd(out=ctxT_o[j][:, qt * 512:(qt + 1) * 512], in_=cst[:])
                    dd(out=rs_o[j:j + 1, qt * 512:(qt + 1) * 512], in_=rst[:])
        p.drain()
    es.close()
    return nc


def build_l1ffn_post():
    nc = bass.Bass("TRN2", target_bir_lowering=False)
    xin = nc.dram_tensor("xin", [TC, D], F32, kind="ExternalInput").ap()
    ct_d = nc.dram_tensor("ct", [8, 128, TC], BF16, kind="ExternalInput").ap()
    go_d = nc.dram_tensor("go", [TC, 64], F32, kind="ExternalInput").ap()
    rs_d = nc.dram_tensor("rs", [TC, 8], F32, kind="ExternalInput").ap()
    ow_d = nc.dram_tensor("ow", [64, 128, D], F32, kind="ExternalInput").ap()
    repl_d = nc.dram_tensor("repl", [64, 64 * 128], BF16, kind="ExternalInput").ap()
    idb_d = nc.dram_tensor("idb", [128, 128], BF16, kind="ExternalInput").ap()
    xout = nc.dram_tensor("xout", [TC, D], F32, kind="ExternalOutput").ap()

    NT16 = TC // 128
    es = ExitStack()
    sb = lambda *a: es.enter_context(nc.sbuf_tensor(*a))
    ps = lambda *a: es.enter_context(nc.psum_tensor(*a))
    idb = sb("idb_sb", [128, 128], BF16)
    repl = sb("repl_sb", [64, 64 * 128], BF16)
    ctx = sb("ctx", [128, 8, TC], BF16)
    go4 = sb("go4", [128, NT16, 8, 8], F32)
    rcp = sb("rcp", [128, NT16, 8], F32)
    gob = sb("gob", [128, NT16, 64], BF16)
    goT = sb("goT", [64, TC], BF16)
    cT = sb("cT", [128, 64, 512], BF16)
    gbs = sb("gbs", [128, 512], BF16)
    owt = sb("owt", [128, 2, 512], BF16)
    oTsb = sb("oTsb", [128, 8, 512], BF16)
    xt = sb("xt", [128, 2, D], F32)
    xo = sb("xo", [128, 2, D], F32)
    odps = ps("odps", [128, 4, 512], F32)
    gbp = ps("gbp", [128, 512], F32)
    tpA = ps("tpA", [128, 128], BF16)
    tpB = ps("tpB", [128, 128], BF16)
    with nc.Block() as blk:
        p = Prog(nc, blk)
        r_const, r_ctx, r_go4, r_rcp, r_gob, r_goT = Res(), Res(), Res(), Res(), Res(), Res()
        r_cT, r_gbs, r_oTsb = Res(), Res(), Res()
        r_owt = [Res(), Res()]
        r_xt = [Res(), Res()]
        r_xo = [Res(), Res()]
        r_odps, r_gbp = Res(), Res()
        r_tp = [Res(), Res()]
        tp = [tpA, tpB]

        @p.step("sync", writes=[r_const, r_ctx], dma=True)
        def _(e, dd):
            dd(out=idb[:], in_=idb_d[:, :])
            dd(out=repl[:], in_=repl_d[:, :])
            dd(out=ctx[:], in_=ct_d.rearrange("c p q -> p c q"))

        @p.step("gpsimd", writes=[r_go4, r_rcp], dma=True)
        def _(e, dd):
            dd(out=go4[:], in_=go_d.rearrange("(t p) (h e) -> p t h e", p=128, e=8))
            dd(out=rcp[:], in_=rs_d.rearrange("(t p) h -> p t h", p=128))

        @p.step("vector", reads=[r_go4, r_rcp], writes=[r_rcp, r_gob])
        def _(e):
            ch = p.chain(e)
            ch(e.reciprocal(rcp[:], rcp[:]))
            rb = rcp[:].unsqueeze(3).broadcast_to([128, NT16, 8, 8])
            return e.tensor_tensor(gob[:].rearrange("p t (h e) -> p t h e", e=8),
                                   go4[:], rb, ALU.mult)
        for t in range(NT16):
            ti = t % 2
            @p.step("tensor", reads=[r_gob, r_const], writes=[r_tp[ti]])
            def _(e, t=t, ti=ti):
                return e.transpose(tp[ti][0:64, :], gob[:, t, :], idb[:])
            @p.step("vector", reads=[r_tp[ti]], writes=[r_goT])
            def _(e, t=t, ti=ti):
                return e.tensor_copy(goT[:, t * 128:(t + 1) * 128], tp[ti][0:64, :])

        for tsl in range(4):
            for he in range(64):
                h = he // 8
                @p.step("tensor", reads=[r_goT, r_const], writes=[r_gbp])
                def _(e, he=he, tsl=tsl):
                    return e.matmul(gbp[:], repl[:, he * 128:(he + 1) * 128],
                                    goT[:, tsl * 512:(tsl + 1) * 512], start=True, stop=True)
                @p.step("vector", reads=[r_gbp], writes=[r_gbs])
                def _(e):
                    return e.tensor_copy(gbs[:], gbp[:])
                @p.step("vector", reads=[r_gbs, r_ctx], writes=[r_cT])
                def _(e, he=he, h=h, tsl=tsl):
                    return e.tensor_tensor(cT[:, he, :], ctx[:, h, tsl * 512:(tsl + 1) * 512],
                                           gbs[:], ALU.mult)
            for dh in range(2):
                for he in range(64):
                    bo = he % 2
                    @p.step("gpsimd", writes=[r_owt[bo]], dma=True)
                    def _(e, dd, he=he, dh=dh, bo=bo):
                        dd(out=owt[:, bo], in_=ow_d[he][:, dh * 512:(dh + 1) * 512])
                    @p.step("tensor", reads=[r_owt[bo], r_cT], writes=[r_odps])
                    def _(e, he=he, dh=dh, bo=bo):
                        for ds in range(4):
                            last = e.matmul(odps[:, ds, :], owt[:, bo, ds * 128:(ds + 1) * 128],
                                            cT[:, he, :], start=(he == 0), stop=(he == 63))
                        return last
                @p.step("vector", reads=[r_odps], writes=[r_oTsb])
                def _(e, dh=dh):
                    return e.tensor_copy(oTsb[:, 4 * dh:4 * dh + 4, :], odps[:])
            for tt in range(4):
                t = tsl * 4 + tt
                bi = t % 2
                @p.step("sync", writes=[r_xt[bi]], dma=True)
                def _(e, dd, t=t, bi=bi):
                    dd(out=xt[:, bi, :], in_=xin[t * 128:(t + 1) * 128, :])
                for d in range(8):
                    ti = d % 2
                    @p.step("tensor", reads=[r_oTsb, r_const], writes=[r_tp[ti]])
                    def _(e, d=d, tt=tt, ti=ti):
                        return e.transpose(tp[ti][:], oTsb[:, d, tt * 128:(tt + 1) * 128], idb[:])
                    @p.step("vector", reads=[r_tp[ti], r_xt[bi]], writes=[r_xo[bi]])
                    def _(e, d=d, ti=ti, bi=bi):
                        return e.tensor_tensor(xo[:, bi, d * 128:(d + 1) * 128], tp[ti][:],
                                               xt[:, bi, d * 128:(d + 1) * 128], ALU.add)
                @p.step("gpsimd", reads=[r_xo[bi]], dma=True)
                def _(e, dd, t=t, bi=bi):
                    dd(out=xout[t * 128:(t + 1) * 128, :], in_=xo[:, bi, :])
        p.drain()
    es.close()
    return nc


def run_l1ffn(x_full, q_w, k_w, v_w, o_w, sel_v_w, sel_o_w, route_scale):
    nc = _get("l1ffn_pre", build_l1ffn_pre)
    xs = x_full.reshape(8, TC, D)
    selw = np.ascontiguousarray(np.concatenate([sel_v_w, sel_o_w], 0), dtype=np.float32)
    rscr = np.broadcast_to(route_scale.reshape(1, 1), (128, 1)).astype(np.float32)
    im = [{"xin": np.ascontiguousarray(xs[c]), "qw": q_w, "kw": k_w, "selw": selw,
           "vw": v_w, "rsc": rscr, "idb": _EYE_BF} for c in range(NCORES)]
    r1 = run_bass_kernel_spmd(nc, im, core_ids=list(range(NCORES))).results
    qT = np.zeros((2, 8, 128, T), BF)
    kT = np.zeros((2, 8, 128, T), BF)
    Vv = np.zeros((2, T, 8, 129), BF)
    Vv[:, :, :, 128] = 1.0
    go = np.zeros((2, T, 64), np.float32)
    for c in range(NCORES):
        b, sl = c // 4, c % 4
        cols = slice(sl * TC, (sl + 1) * TC)
        for h in range(8):
            qT[b, h][:, cols] = r1[c]["qT_o"][h]
            kT[b, h][:, cols] = r1[c]["kT_o"][h]
        Vv[b, cols.start:cols.stop, :, 0:128] = r1[c]["V_o"]
        go[b, cols.start:cols.stop] = r1[c]["go_o"]
    nc2 = _get("l1ffn_flash", build_l1ffn_flash)
    im2 = []
    for c in range(NCORES):
        prs = [2 * c, 2 * c + 1]
        im2.append({"qT": np.stack([qT[pp // 8, pp % 8] for pp in prs]),
                    "kT": np.stack([kT[pp // 8, pp % 8] for pp in prs]),
                    "V": np.stack([np.ascontiguousarray(Vv[pp // 8, :, pp % 8]) for pp in prs])})
    r2 = run_bass_kernel_spmd(nc2, im2, core_ids=list(range(NCORES))).results
    ctxT = np.zeros((2, 8, 128, T), BF)
    rs = np.zeros((2, 8, T), np.float32)
    for c in range(NCORES):
        for jj in range(2):
            pp = 2 * c + jj
            ctxT[pp // 8, pp % 8] = r2[c]["ctxT_o"][jj]
            rs[pp // 8, pp % 8] = r2[c]["rs_o"][jj]
    nc3 = _get("l1ffn_post", build_l1ffn_post)
    repl = np.zeros((64, 64 * 128), np.float32)
    for he in range(64):
        repl[he, he * 128:(he + 1) * 128] = 1.0
    repl = repl.astype(BF)
    im3 = []
    for c in range(NCORES):
        b, sl = c // 4, c % 4
        cols = slice(sl * TC, (sl + 1) * TC)
        im3.append({"xin": np.ascontiguousarray(xs[c]),
                    "ct": np.ascontiguousarray(ctxT[b][:, :, cols]),
                    "go": np.ascontiguousarray(go[b, cols.start:cols.stop]),
                    "rs": np.ascontiguousarray(rs[b][:, cols.start:cols.stop].T),
                    "ow": o_w, "repl": repl, "idb": _EYE_BF})
    r3 = run_bass_kernel_spmd(nc3, im3, core_ids=list(range(NCORES))).results
    return np.stack([r["xout"] for r in r3]).reshape(2, 8192, 1024)


_BUILD_CACHE = {}


def _get(name, builder):
    if name not in _BUILD_CACHE:
        _BUILD_CACHE[name] = builder()
    return _BUILD_CACHE[name]


_EYE_BF = np.eye(128).astype(BF)
_EYE_F32 = np.eye(128).astype(np.float32)


def run_l0ffn(x_full, wg, wu, wd):
    nc = _get("l0ffn", build_l0ffn)
    xs = x_full.reshape(8, TC, D)
    in_maps = [{"xin": np.ascontiguousarray(xs[c]), "wg": wg, "wu": wu, "wd": wd,
                "idb": _EYE_BF} for c in range(NCORES)]
    res = run_bass_kernel_spmd(nc, in_maps, core_ids=list(range(NCORES)))
    out = np.stack([r["xout"] for r in res.results]).reshape(2, 8192, 1024)
    return out


def kernel(**inputs):
    x = np.ascontiguousarray(inputs["L0_attn_x"], dtype=np.float32)
    x1 = run_l0attn(x, inputs["L0_attn_Wqkv"], inputs["L0_attn_Wout"], inputs["L0_attn_bout"])
    x2 = run_l0ffn(x1, inputs["L0_ffn_w_gate"], inputs["L0_ffn_w_up"], inputs["L0_ffn_w_down"])
    x3 = run_l1attn(x2, inputs["L1_attn_Wq"], inputs["L1_attn_Wk"], inputs["L1_attn_Wv"],
                    inputs["L1_attn_Wo"], inputs["L1_attn_q_gamma"], inputs["L1_attn_q_beta"],
                    inputs["L1_attn_k_gamma"], inputs["L1_attn_k_beta"])
    x4 = run_l1ffn(x3, inputs["L1_ffn_q_w"], inputs["L1_ffn_k_w"], inputs["L1_ffn_v_w"],
                   inputs["L1_ffn_o_w"], inputs["L1_ffn_sel_v_w"], inputs["L1_ffn_sel_o_w"],
                   inputs["L1_ffn_route_scale"])
    return x4
